# version 01 (final)
# baseline (speedup 1.0000x reference)
import math
import numpy as np
import concourse.bass as bass
import concourse.mybir as mybir
from concourse.bass_utils import run_bass_kernel_spmd
from concourse.alu_op_type import AluOpType as OP

F32 = mybir.dt.float32
BF16 = mybir.dt.bfloat16
AF = mybir.ActivationFunctionType
AX = mybir.AxisListType

TOK = 2048
NG = 8
GS = 256
D = 1024
E = 2048
H = 4096
QKD = 128
EPS = 1e-5
SQ = float(math.sqrt(128.0))

ENGS = ("sync", "tensor", "vector", "scalar", "gpsimd")
DEBUG = False


class KB:
    def __init__(self):
        self.nc = bass.Bass("TRN2", target_bir_lowering=False)
        self.ops = {e: [] for e in ENGS}
        self.sems = {}
        self.cnt = {}
        for nm in ("LD", "MM", "V", "A", "G", "ST", "TT", "CC"):
            self.sems[nm] = self.nc.alloc_semaphore("s" + nm)
            self.cnt[nm] = 0
        self.bank_guard = {}
        self._waited = {e: {} for e in ENGS}

    def emit(self, eng, fn):
        self.ops[eng].append(fn)

    def wait(self, eng, nm, n=None):
        n = self.cnt[nm] if n is None else n
        if n <= 0 or n <= self._waited[eng].get(nm, 0):
            return
        self._waited[eng][nm] = n
        s = self.sems[nm]
        self.emit(eng, lambda e, s=s, n=n: e.wait_ge(s, n))

    def _sig(self, instr_fn, eng, nm, k):
        if nm is None:
            self.emit(eng, instr_fn)
            return None
        self.cnt[nm] += k
        s = self.sems[nm]
        self.emit(eng, lambda e, f=instr_fn, s=s, k=k: f(e).then_inc(s, k))
        return self.cnt[nm]

    def ld(self, dst, src):
        return self._sig(lambda e, d=dst, s=src: e.dma_start(out=d, in_=s),
                         "sync", "LD", 16)

    def st(self, dst, src):
        return self._sig(lambda e, d=dst, s=src: e.dma_start(out=d, in_=s),
                         "gpsimd", "ST", 16)

    def trdma(self, dst, src):
        return self._sig(lambda e, d=dst, s=src: e.dma_start(out=d, in_=s, transpose=True),
                         "sync", "TT", 16)

    def mm(self, out, lhsT, rhs, start, stop, sig=None):
        return self._sig(
            lambda e, o=out, l=lhsT, r=rhs, a=start, b=stop: e.matmul(o, l, r, start=a, stop=b),
            "tensor", sig, 1)

    def tp(self, out, in_, ident, sig=None):
        return self._sig(lambda e, o=out, i=in_, d=ident: e.transpose(o, i, d),
                         "tensor", sig, 1)

    def v_ts(self, out, in0, s1, s2, op0, op1=None, sig=None):
        if op1 is None:
            return self._sig(lambda e, o=out, i=in0, a=s1, p=op0: e.tensor_scalar(o, i, a, None, p),
                             "vector", sig, 1)
        return self._sig(lambda e, o=out, i=in0, a=s1, b=s2, p=op0, q=op1: e.tensor_scalar(o, i, a, b, p, q),
                         "vector", sig, 1)

    def v_tt(self, out, in0, in1, op, sig=None):
        return self._sig(lambda e, o=out, a=in0, b=in1, p=op: e.tensor_tensor(o, a, b, p),
                         "vector", sig, 1)

    def v_stt(self, out, in0, s, in1, op0, op1, sig=None):
        return self._sig(lambda e, o=out, a=in0, c=s, b=in1, p=op0, q=op1: e.scalar_tensor_tensor(o, a, c, b, p, q),
                         "vector", sig, 1)

    def v_copy(self, out, in_, sig=None):
        return self._sig(lambda e, o=out, i=in_: e.tensor_copy(o, i), "vector", sig, 1)

    def v_red(self, out, in_, op, absval=False, sig=None):
        return self._sig(
            lambda e, o=out, i=in_, p=op, a=absval: e.tensor_reduce(o, i, AX.X, p, apply_absolute_value=a),
            "vector", sig, 1)

    def v_recip(self, out, in_, sig=None):
        return self._sig(lambda e, o=out, i=in_: e.reciprocal(o, i), "vector", sig, 1)

    def v_bn(self, out, in_, sig=None):
        return self._sig(lambda e, o=out, i=in_: e.bn_stats(o, i), "vector", sig, 1)

    def v_bnagg(self, out, in_, sig=None):
        return self._sig(lambda e, o=out, i=in_: e.bn_aggr(o, i), "vector", sig, 1)

    def act(self, out, in_, func, scale=None, bias=None, sig=None):
        kw = {}
        if scale is not None:
            kw["scale"] = scale
        if bias is not None:
            kw["bias"] = bias
        return self._sig(lambda e, o=out, i=in_, f=func, k=kw: e.activation(out=o, in_=i, func=f, **k),
                         "scalar", sig, 1)

    def use_banks(self, banks):
        for b in banks:
            g = self.bank_guard.get(b)
            if g is not None:
                self.wait("tensor", g[0], g[1])

    def evac_done(self, banks, nm, n):
        for b in banks:
            self.bank_guard[b] = (nm, n)

    def barrier(self):
        snap = {nm: self.cnt[nm] for nm in ("LD", "MM", "V", "A", "G", "ST", "TT")}
        owner = {"LD": "sync", "MM": "tensor", "V": "vector", "A": "scalar",
                 "G": "gpsimd", "ST": "gpsimd", "TT": "sync"}
        for eng in ENGS:
            for nm, n in snap.items():
                if owner[nm] != eng:
                    self.wait(eng, nm, n)


def build_program():
    kb = KB()
    nc = kb.nc

    dI = lambda nm, shape, dt=F32: nc.dram_tensor(nm, shape, dt, kind="ExternalInput").ap()
    x_d = dI("x", [TOK, D])
    wqk_d = dI("wqk", [D, QKD])
    whv_d = dI("whv", [D, E])
    whg_d = dI("whg", [D, E])
    wo_d = dI("wo", [E, D])
    w1t_d = dI("w1t", [D, H])
    w2t_d = dI("w2t", [H, D])
    bqk_d = dI("bqk", [QKD, 1])
    gqk_d = dI("gqk", [QKD, 4])
    betaqk_d = dI("betaqk", [QKD, 4])
    bhv_rep_d = dI("bhv_rep", [128, E])
    bhg_cols_d = dI("bhg_cols", [128, 16])
    biasT_d = dI("biasT", [GS, GS])
    ident_d = dI("ident", [128, 128])
    ones128_d = dI("ones128", [128, 128])
    g2rep_d = dI("g2rep", [128, D])
    b2rep_d = dI("b2rep", [128, D])
    g1rep_d = dI("g1rep", [128, D])
    b1r_d = dI("b1r", [128, D])
    b1rep_d = dI("b1rep", [128, H])
    g4rep_d = dI("g4rep", [128, H])
    b4rep_d = dI("b4rep", [128, H])
    b2o_rep_d = dI("b2o_rep", [128, D])
    bo_rep_d = dI("bo_rep", [128, D])

    y_d = nc.dram_tensor("y", [TOK, D], F32, kind="ExternalOutput").ap()

    if DEBUG:
        dS = lambda nm, shape, dt=F32: nc.dram_tensor(nm, shape, dt, kind="ExternalOutput").ap()
    else:
        dS = lambda nm, shape, dt=F32: nc.dram_tensor(nm, shape, dt).ap()
    xn_sp = dS("xn_sp", [16, 128, D])
    v_sp = dS("v_sp", [NG, 2, 128, E])
    at_sp = dS("at_sp", [NG, 2, 128, GS])
    t_sp = dS("t_sp", [NG, 16, 128, GS])
    y1_sp = dS("y1_sp", [16, 128, D])
    qx2_sp = dS("qx2_sp", [16, 128, H], BF16)
    cc_in = nc.dram_tensor("cc_in", [128, E], F32).ap()
    cc_out = nc.dram_tensor("cc_out", [128, E], F32).ap()
    lkv_dbg = dS("lkv_dbg", [128, E]) if DEBUG else None
    qx1_dbg = dS("qx1_dbg", [16, 128, D], BF16) if DEBUG else None
    h_dbg = dS("h_dbg", [16, 128, H]) if DEBUG else None
    wsv_dbg = dS("wsv_dbg", [128, 4]) if DEBUG else None

    sbp = lambda nm, shape, dt=F32: nc.alloc_sbuf_tensor(nm, shape, dt).ap()
    ident = sbp("ident_s", [128, 128])
    ones128 = sbp("ones128_s", [128, 128])
    biasT = sbp("biasT_s", [128, 2, GS])
    linq = sbp("linq", [128, TOK])
    linkv = sbp("linkv", [128, E])
    sc2all = sbp("sc2all", [128, 16])
    wsv = sbp("wsv", [128, 4])
    stat = sbp("stat", [128, 52])
    svec = sbp("svec", [128, 8])
    bqk = sbp("bqk_s", [128, 1])
    gqk = sbp("gqk_s", [128, 4])
    betaqk = sbp("betaqk_s", [128, 4])
    bhg_cols = sbp("bhg_cols_s", [128, 16])

    TOPB = nc.sbuf_top
    AR_SZ = 114688
    W_SZ = 65536
    AR_OFF = TOPB - AR_SZ
    W_OFF = AR_OFF - W_SZ
    assert nc.sbuf_base <= W_OFF, (nc.sbuf_base, W_OFF)

    sba = lambda nm, shape, dt, off: nc.alloc_sbuf_tensor_at(nm, shape, dt, offset=off).ap()

    wbig = sba("wbig", [128, 8, E], F32, W_OFF)
    wbig16 = sba("wbig16", [128, 8, H], BF16, W_OFF)
    wbig_f = wbig.rearrange("p k e -> p (k e)")
    wo_v = wbig_f.rearrange("p (k dd) -> p k dd", k=16)
    wbig16_f = wbig16.rearrange("p k h -> p (k h)")
    qw2_v = wbig16_f.rearrange("p (k dd) -> p k dd", k=32)

    o = AR_OFF
    xg = sba("xg", [128, 2, D], F32, o + 0)
    xng = sba("xng", [128, 2, D], F32, o + 8192)
    zg = sba("zg", [128, 2, D], F32, o + 16384)
    nrm = sba("nrm", [128, 2, D], F32, o + 24576)
    nrmT = sba("nrmT", [128, 8, GS], F32, o + 32768)
    vg = sba("vg", [128, 2, E], F32, o + 40960)
    qkf = sba("qkf", [128, GS], F32, o + 57344)
    heads = sba("heads", [128, 4, GS], F32, o + 58368)
    linkT = sba("linkT", [128, 2, 128], F32, o + 62464)
    attnT = sba("attnT", [128, 2, GS], F32, o + 63488)
    atmp = sba("atmp", [128, 2, GS], F32, o + 65536)
    wqk = sba("wqk_s", [128, 8, 128], F32, o + 67584)
    g2rep = sba("g2rep_s", [128, D], F32, o + 71680)
    b2rep = sba("b2rep_s", [128, D], F32, o + 75776)
    bhv_rep = sba("bhv_rep_s", [128, E], F32, o + 79872)
    lnsqA = sba("lnsqA", [128, D], F32, o + 88064)

    wchunk = sba("wchunk", [128, H], F32, o + 0)
    hbufW = sba("hbufW", [128, H], F32, o + 16384)
    wqbufW = sba("wqbufW", [128, H], F32, o + 32768)
    wacc = sba("wacc", [128, 1], F32, o + 49152)
    wpart = sba("wpart", [128, 32], F32, o + 49280)

    xnc = sba("xnc", [128, D], F32, o + 0)
    hbuf = sba("hbuf", [128, H], F32, o + 4096)
    wqbuf = sba("wqbuf", [128, H], F32, o + 20480)
    ibuf = sba("ibuf", [128, D], F32, o + 36864)
    g1r = sba("g1r", [128, D], F32, o + 40960)
    b1r = sba("b1r_s", [128, D], F32, o + 45056)
    b1rep = sba("b1rep_s", [128, H], F32, o + 49152)
    g4rep = sba("g4rep_s", [128, H], F32, o + 65536)
    b4rep = sba("b4rep_s", [128, H], F32, o + 81920)
    qx1 = sba("qx1", [128, D], BF16, o + 98304)
    qx1T = sba("qx1T", [128, 8, 128], BF16, o + 100352)
    qx2 = sba("qx2", [128, H], BF16, o + 102400)

    xngB = sba("xngB", [128, 2, D], F32, o + 0)
    zgB = sba("zgB", [128, 2, D], F32, o + 8192)
    nrmB = sba("nrmB", [128, 2, D], F32, o + 16384)
    nrmTB = sba("nrmTB", [128, 8, GS], F32, o + 24576)
    vgB = sba("vgB", [128, 2, E], F32, o + 32768)
    attnTB = sba("attnTB", [128, 2, GS], F32, o + 49152)
    g2B = sba("g2B", [128, D], F32, o + 51200)
    b2B = sba("b2B", [128, D], F32, o + 55296)
    gateB = sba("gateB", [128, 16, GS], F32, o + 59392)
    tgB = sba("tgB", [128, 16, GS], F32, o + 75776)
    lnsqB = sba("lnsqB", [128, D], F32, o + 92160)

    tg2 = sba("tg2", [128, 16, GS], F32, o + 0)
    xg2 = sba("xg2", [128, 2, D], F32, o + 16384)
    xng2 = sba("xng2", [128, 2, D], F32, o + 24576)
    y1g = sba("y1g", [128, 2, D], F32, o + 32768)
    borep = sba("borep", [128, D], F32, o + 40960)

    qx2f = sba("qx2f", [128, H], BF16, o + 0)
    qx2T = sba("qx2T", [128, 32, 128], BF16, o + 8192)
    y1f = sba("y1f", [128, D], F32, o + 16384)
    youtf = sba("youtf", [128, D], F32, o + 20480)
    b2orep = sba("b2orep", [128, D], F32, o + 24576)

    psum = nc.alloc_psum_tensor("psum", [128, 4096], F32).ap()

    def BANK(b, w=512):
        return psum[:, b * 512:b * 512 + w]

    def ln_stats(src2d, width, sq_ap):
        kb.wait("scalar", "V")
        kb.wait("scalar", "LD")
        kb.act(sq_ap, src2d, AF.Square, sig="A")
        kb.v_red(stat[:, 0:1], src2d, OP.add, sig="V")
        kb.wait("vector", "V")
        kb.wait("vector", "A")
        kb.v_red(stat[:, 1:2], sq_ap, OP.add, sig="V")
        kb.wait("vector", "V")
        kb.v_ts(stat[:, 2:3], stat[:, 0:1], 1.0 / width, None, OP.mult, sig="V")
        kb.wait("vector", "V")
        kb.v_ts(stat[:, 3:4], stat[:, 1:2], 1.0 / width, None, OP.mult, sig="V")
        kb.wait("vector", "V")
        kb.v_stt(stat[:, 4:5], stat[:, 2:3], stat[:, 2:3], stat[:, 3:4],
                 OP.mult, OP.subtract, sig="V")
        kb.wait("vector", "V")
        kb.v_ts(svec[:, 6:7], stat[:, 4:5], -1.0, EPS, OP.mult, OP.add, sig="V")
        kb.wait("vector", "V")
        kb.v_recip(svec[:, 7:8], svec[:, 6:7], sig="V")
        kb.wait("scalar", "V")
        kb.act(svec[:, 3:4], svec[:, 7:8], AF.Sqrt, sig="A")
        kb.wait("scalar", "A")
        kb.wait("vector", "A")
        kb.v_ts(svec[:, 4:5], stat[:, 2:3], svec[:, 3:4], -1.0, OP.mult, OP.mult, sig="V")
        kb.wait("scalar", "V")

    def cascade(w0_ap, wr_ap, ind_ap):
        first = True
        for t in (128.0, 64.0, 32.0, 16.0, 8.0, 4.0, 2.0, 1.0):
            src = w0_ap if first else wr_ap
            kb.v_ts(ind_ap, src, t, t, OP.is_ge, OP.mult)
            kb.v_tt(wr_ap, src, ind_ap, OP.subtract)
            first = False

    def quantize_act(qin_ap, w0_ap, wr_ap, qx_ap, width, mc_store=None):
        kb.v_red(svec[:, 0:1], qin_ap, OP.max, absval=True, sig="V")
        kb.wait("vector", "V")
        kb.v_ts(svec[:, 0:1], svec[:, 0:1], 1e-5, None, OP.max, sig="V")
        kb.wait("vector", "V")
        if mc_store is not None:
            kb.v_copy(mc_store, svec[:, 0:1])
        kb.v_recip(svec[:, 1:2], svec[:, 0:1], sig="V")
        kb.wait("vector", "V")
        kb.v_ts(svec[:, 2:3], svec[:, 1:2], 127.0, None, OP.mult, sig="V")
        kb.wait("vector", "V")
        kb.v_ts(w0_ap, qin_ap, svec[:, 2:3], 128.5, OP.mult, OP.add)
        nh = max(1, width // D)
        cw = width // nh
        for hh in range(nh):
            sl = slice(hh * cw, (hh + 1) * cw)
            cascade(w0_ap[:, sl], wr_ap[:, sl], ibuf[:, 0:cw])
        kb.v_stt(qx_ap, w0_ap, -128.0, wr_ap, OP.add, OP.subtract, sig="V")

    def quantize_weight(wt_dram, nchunks, cw, out16_slice, wsv_col):
        for k in range(nchunks):
            kb.wait("sync", "V")
            kb.ld(wchunk[:, 0:cw], wt_dram[k])
            kb.wait("vector", "LD")
            kb.v_red(wpart[:, k:k + 1], wchunk[:, 0:cw], OP.add, absval=True, sig="V")
        kb.wait("vector", "V")
        kb.v_red(wacc[:], wpart[:, 0:nchunks], OP.add, sig="V")
        kb.wait("vector", "V")
        kb.wait("tensor", "V")
        kb.use_banks([0])
        n_mm = kb.mm(BANK(0, 1), ones128[:], wacc[:], start=True, stop=True, sig="MM")
        kb.wait("vector", "MM", n_mm)
        kb.v_ts(wsv[:, wsv_col:wsv_col + 1], BANK(0, 1), 1.0 / (H * D), 1e-5,
                OP.mult, OP.max, sig="V")
        kb.wait("vector", "V")
        kb.v_recip(wsv[:, wsv_col + 1:wsv_col + 2], wsv[:, wsv_col:wsv_col + 1], sig="V")
        kb.wait("vector", "V")
        n_v = kb.cnt["V"]
        kb.evac_done([0], "V", n_v)
        swv = wsv[:, wsv_col + 1:wsv_col + 2]
        for k in range(nchunks):
            kb.wait("sync", "V")
            kb.ld(wchunk[:, 0:cw], wt_dram[k])
            kb.wait("vector", "LD")
            kb.v_ts(hbufW[:, 0:cw], wchunk[:, 0:cw], swv, 0.5, OP.mult, OP.is_ge)
            kb.v_ts(wqbufW[:, 0:cw], wchunk[:, 0:cw], swv, -0.5, OP.mult, OP.is_le)
            kb.v_tt(out16_slice(k), hbufW[:, 0:cw], wqbufW[:, 0:cw], OP.subtract, sig="V")

    def emit_normed(xg_ap, xng_ap, zg_ap, nrm_ap, g2_ap, b2_ap, with_xn, sq_ap):
        for c in range(2):
            if with_xn:
                ln_stats(xg_ap[:, c, :], D, sq_ap)
                kb.act(xng_ap[:, c, :], xg_ap[:, c, :], AF.Identity,
                       scale=svec[:, 3:4], bias=svec[:, 4:5], sig="A")
                kb.wait("vector", "A")
            ln_stats(xng_ap[:, c, :], D, sq_ap)
            kb.act(zg_ap[:, c, :], xng_ap[:, c, :], AF.Identity,
                   scale=svec[:, 3:4], bias=svec[:, 4:5], sig="A")
            kb.wait("vector", "A")
            kb.v_tt(nrm_ap[:, c, :], zg_ap[:, c, :], g2_ap, OP.mult)
            kb.v_tt(nrm_ap[:, c, :], nrm_ap[:, c, :], b2_ap, OP.add, sig="V")

    def emit_transpose_1k(nrm_ap, nrmT_ap):
        kb.wait("tensor", "V")
        for c in range(2):
            for k in range(8):
                kb.use_banks([0])
                n_mm = kb.tp(BANK(0, 128), nrm_ap[:, c, k * 128:(k + 1) * 128], ident[:], sig="MM")
                kb.wait("scalar", "MM", n_mm)
                n_a = kb.act(nrmT_ap[:, k, c * 128:(c + 1) * 128], BANK(0, 128), AF.Identity, sig="A")
                kb.evac_done([0], "A", n_a)

    kb.ld(ident[:], ident_d[:, :])
    kb.ld(ones128[:], ones128_d[:, :])
    kb.ld(biasT[:], biasT_d.rearrange("(a p) q -> p a q", p=128))
    kb.ld(wqk[:], wqk_d.rearrange("(k p) j -> p k j", p=128))
    kb.ld(bqk[:], bqk_d[:, :])
    kb.ld(gqk[:], gqk_d[:, :])
    kb.ld(betaqk[:], betaqk_d[:, :])
    kb.ld(g2rep[:], g2rep_d[:, :])
    kb.ld(b2rep[:], b2rep_d[:, :])
    kb.ld(bhv_rep[:], bhv_rep_d[:, :])
    kb.ld(bhg_cols[:], bhg_cols_d[:, :])
    kb.ld(wbig[:], whv_d.rearrange("(k p) e -> p k e", p=128))
    kb.wait("vector", "LD")
    kb.v_ts(biasT[:, 0, :], biasT[:, 0, :], SQ, None, OP.mult)
    kb.v_ts(biasT[:, 1, :], biasT[:, 1, :], SQ, None, OP.mult, sig="V")

    x_r = x_d.rearrange("(g c p) d -> g p c d", c=2, p=128)
    for g in range(NG):
        kb.barrier()
        kb.ld(xg[:], x_r[g])
        kb.wait("vector", "LD")
        emit_normed(xg, xng, zg, nrm, g2rep[:], b2rep[:], with_xn=True, sq_ap=lnsqA[:])
        kb.wait("gpsimd", "A")
        kb.st(xn_sp[2 * g], xng[:, 0, :])
        kb.st(xn_sp[2 * g + 1], xng[:, 1, :])
        emit_transpose_1k(nrm, nrmT)
        kb.use_banks([1])
        kb.wait("tensor", "A")
        for k in range(8):
            n_mm = kb.mm(BANK(1, GS), wqk[:, k, :], nrmT[:, k, :],
                         start=(k == 0), stop=(k == 7), sig="MM" if k == 7 else None)
        kb.wait("scalar", "MM", n_mm)
        n_a = kb.act(qkf[:], BANK(1, GS), AF.Silu, scale=1.0, bias=bqk[:], sig="A")
        kb.evac_done([1], "A", n_a)
        kb.wait("vector", "A", n_a)
        for hh in range(4):
            kb.v_ts(heads[:, hh, :], qkf[:], gqk[:, hh:hh + 1], betaqk[:, hh:hh + 1],
                    OP.mult, OP.add, sig="V" if hh == 3 else None)
        kb.v_copy(linq[:, g * GS:(g + 1) * GS], heads[:, 1, :], sig="V")
        kb.wait("tensor", "V")
        for kc in range(2):
            kb.use_banks([1 + kc])
            n_mm = kb.mm(BANK(1 + kc, GS), heads[:, 2, kc * 128:(kc + 1) * 128], heads[:, 0, :],
                         start=True, stop=True, sig="MM")
            kb.wait("vector", "MM", n_mm)
            kb.v_stt(atmp[:, kc, :], BANK(1 + kc, GS), 1.0 / GS, biasT[:, kc, :], OP.mult, OP.add, sig="V")
            kb.wait("vector", "V")
            n_v = kb.v_ts(atmp[:, kc, :], atmp[:, kc, :], 0.0, None, OP.max, sig="V")
            kb.evac_done([1 + kc], "V", n_v)
            kb.wait("scalar", "V", n_v)
            kb.act(attnT[:, kc, :], atmp[:, kc, :], AF.Square, sig="A")
        for kc in range(2):
            kb.use_banks([3])
            n_mm = kb.tp(BANK(3, 128), heads[:, 3, kc * 128:(kc + 1) * 128], ident[:], sig="MM")
            kb.wait("scalar", "MM", n_mm)
            n_a = kb.act(linkT[:, kc, :], BANK(3, 128), AF.Identity, sig="A")
            kb.evac_done([3], "A", n_a)
        kb.wait("tensor", "A")
        for c in range(2):
            for eb in range(4):
                b = 2 + ((c * 4 + eb) % 2)
                kb.use_banks([b])
                for k in range(8):
                    n_mm = kb.mm(BANK(b), nrmT[:, k, c * 128:(c + 1) * 128],
                                 wbig[:, k, eb * 512:(eb + 1) * 512],
                                 start=(k == 0), stop=(k == 7), sig="MM" if k == 7 else None)
                kb.wait("vector", "MM", n_mm)
                n_v = kb.v_stt(vg[:, c, eb * 512:(eb + 1) * 512], BANK(b), 1.0,
                               bhv_rep[:, eb * 512:(eb + 1) * 512], OP.mult, OP.add, sig="V")
                kb.evac_done([b], "V", n_v)
                kb.wait("scalar", "V", n_v)
                kb.act(vg[:, c, eb * 512:(eb + 1) * 512], vg[:, c, eb * 512:(eb + 1) * 512],
                       AF.Silu, sig="A")
            kb.wait("tensor", "A")
            for eb in range(4):
                kb.mm(psum[:, 2048 + eb * 512:2048 + (eb + 1) * 512],
                      linkT[:, c, :], vg[:, c, eb * 512:(eb + 1) * 512],
                      start=(g == 0 and c == 0), stop=(g == NG - 1 and c == 1),
                      sig="MM" if eb == 3 else None)
        kb.wait("gpsimd", "A")
        kb.st(v_sp[g, 0], vg[:, 0, :])
        kb.st(v_sp[g, 1], vg[:, 1, :])
        kb.st(at_sp[g, 0], attnT[:, 0, :])
        kb.st(at_sp[g, 1], attnT[:, 1, :])

    kb.wait("vector", "MM")
    n_v = kb.v_ts(linkv[:], psum[:, 2048:4096], 1.0 / 4096.0, None, OP.mult, sig="V")
    kb.evac_done([4, 5, 6, 7], "V", n_v)
    kb.wait("gpsimd", "V")
    kb.st(cc_in[:, :], linkv[:])
    kb.wait("gpsimd", "ST")
    kb.cnt["CC"] = 1
    _ccs = kb.sems["CC"]
    kb.emit("gpsimd", lambda e, s=_ccs: e.collective_compute(
        "AllReduce", mybir.AluOpType.add, ins=[cc_in[:, :]], outs=[cc_out[:, :]],
        replica_groups=[[0, 1], [2, 3], [4, 5], [6, 7]]).then_inc(s))

    kb.barrier()
    quantize_weight(w1t_d.rearrange("(k p) h -> k p h", p=128), 8, H,
                    lambda k: wbig16[:, k, :], 0)

    kb.barrier()
    kb.ld(g1r[:], g1rep_d[:, :])
    kb.ld(b1r[:], b1r_d[:, :])
    kb.ld(b1rep[:], b1rep_d[:, :])
    kb.ld(g4rep[:], g4rep_d[:, :])
    kb.ld(b4rep[:], b4rep_d[:, :])
    z1 = hbuf[:, 0:D]
    q1 = hbuf[:, D:2 * D]
    w01 = hbuf[:, 2 * D:3 * D]
    wr1 = hbuf[:, 3 * D:4 * D]
    for g in range(NG):
        for c in range(2):
            idx = 2 * g + c
            kb.barrier()
            kb.ld(xnc[:], xn_sp[idx])
            kb.wait("vector", "LD")
            ln_stats(xnc[:], D, z1)
            kb.act(z1, xnc[:], AF.Identity, scale=svec[:, 3:4], bias=svec[:, 4:5], sig="A")
            kb.wait("vector", "A")
            kb.v_tt(q1, z1, g1r[:], OP.mult)
            kb.v_tt(q1, q1, b1r[:], OP.add)
            quantize_act(q1, w01, wr1, qx1[:], D)
            kb.wait("sync", "V")
            kb.trdma(qx1T[:], qx1[:])
            if DEBUG:
                kb.wait("gpsimd", "V")
                kb.st(qx1_dbg[idx], qx1[:])
            kb.v_ts(svec[:, 5:6], svec[:, 0:1], wsv[:, 0:1], 1.0 / 127.0, OP.mult, OP.mult, sig="V")
            kb.wait("vector", "V")
            kb.wait("tensor", "TT")
            kb.wait("tensor", "V")
            for hb in range(8):
                b = hb % 4
                kb.use_banks([b])
                for k in range(8):
                    n_mm = kb.mm(BANK(b), qx1T[:, k, :],
                                 wbig16[:, k, hb * 512:(hb + 1) * 512],
                                 start=(k == 0), stop=(k == 7), sig="MM" if k == 7 else None)
                kb.wait("vector", "MM", n_mm)
                n_v = kb.v_stt(wqbuf[:, hb * 512:(hb + 1) * 512], BANK(b), svec[:, 5:6],
                               b1rep[:, hb * 512:(hb + 1) * 512], OP.mult, OP.add, sig="V")
                kb.evac_done([b], "V", n_v)
                kb.wait("scalar", "V", n_v)
                kb.act(hbuf[:, hb * 512:(hb + 1) * 512], wqbuf[:, hb * 512:(hb + 1) * 512],
                       AF.Gelu, sig="A")
            kb.wait("vector", "A")
            if DEBUG:
                kb.wait("gpsimd", "A")
                kb.st(h_dbg[idx], hbuf[:])
                kb.wait("vector", "ST")
            ln_stats(hbuf[:], H, wqbuf[:])
            kb.act(hbuf[:], hbuf[:], AF.Identity, scale=svec[:, 3:4], bias=svec[:, 4:5], sig="A")
            kb.wait("vector", "A")
            kb.v_tt(wqbuf[:], hbuf[:], g4rep[:], OP.mult)
            kb.v_tt(wqbuf[:], wqbuf[:], b4rep[:], OP.add)
            quantize_act(wqbuf[:], hbuf[:], wqbuf[:], qx2[:], H,
                         mc_store=sc2all[:, idx:idx + 1])
            kb.wait("gpsimd", "V")
            kb.st(qx2_sp[idx], qx2[:])

    if DEBUG:
        kb.wait("gpsimd", "V")
        kb.st(wsv_dbg[:, :], wsv[:])
    kb.barrier()
    kb.ld(wbig[:], whg_d.rearrange("(k p) e -> p k e", p=128))
    kb.ld(g2B[:], g2rep_d[:, :])
    kb.ld(b2B[:], b2rep_d[:, :])
    kb.wait("sync", "CC", 1)
    kb.ld(linkv[:], cc_out[:, :])
    if DEBUG:
        kb.wait("gpsimd", "LD")
        kb.st(lkv_dbg[:, :], linkv[:])

    for g in range(NG):
        kb.barrier()
        kb.ld(xngB[:, 0, :], xn_sp[2 * g])
        kb.ld(xngB[:, 1, :], xn_sp[2 * g + 1])
        kb.ld(vgB[:], v_sp.rearrange("g c p e -> g p c e")[g])
        kb.ld(attnTB[:], at_sp.rearrange("g c p q -> g p c q")[g])
        kb.wait("vector", "LD")
        emit_normed(None, xngB, zgB, nrmB, g2B[:], b2B[:], with_xn=False, sq_ap=lnsqB[:])
        emit_transpose_1k(nrmB, nrmTB)
        kb.wait("tensor", "A")
        for ec in range(16):
            bgate = ec % 2
            kb.use_banks([bgate])
            for k in range(8):
                n_mm = kb.mm(BANK(bgate, GS), wbig[:, k, ec * 128:(ec + 1) * 128], nrmTB[:, k, :],
                             start=(k == 0), stop=(k == 7), sig="MM" if k == 7 else None)
            kb.wait("scalar", "MM", n_mm)
            n_a = kb.act(gateB[:, ec, :], BANK(bgate, GS), AF.Silu,
                         scale=1.0, bias=bhg_cols[:, ec:ec + 1], sig="A")
            kb.evac_done([bgate], "A", n_a)
            bql = 2 + ec % 2
            kb.use_banks([bql])
            kb.mm(BANK(bql, GS), vgB[:, 0, ec * 128:(ec + 1) * 128], attnTB[:, 0, :],
                  start=True, stop=False)
            kb.mm(BANK(bql, GS), vgB[:, 1, ec * 128:(ec + 1) * 128], attnTB[:, 1, :],
                  start=False, stop=False)
            n_mm = kb.mm(BANK(bql, GS), linkv[:, ec * 128:(ec + 1) * 128],
                         linq[:, g * GS:(g + 1) * GS], start=False, stop=True, sig="MM")
            kb.wait("vector", "MM", n_mm)
            kb.wait("vector", "A", n_a)
            n_v = kb.v_tt(tgB[:, ec, :], BANK(bql, GS), gateB[:, ec, :], OP.mult, sig="V")
            kb.evac_done([bql], "V", n_v)
        kb.wait("gpsimd", "V")
        kb.st(t_sp.rearrange("g a p q -> g p a q")[g], tgB[:])

    kb.barrier()
    kb.ld(wbig_f.rearrange("p (k dd) -> p k dd", k=16), wo_d.rearrange("(k p) dd -> p k dd", p=128))
    kb.ld(borep[:], bo_rep_d[:, :])
    for g in range(NG):
        kb.barrier()
        kb.ld(tg2[:], t_sp.rearrange("g a p q -> g p a q")[g])
        kb.ld(xg2[:], x_r[g])
        kb.ld(xng2[:, 0, :], xn_sp[2 * g])
        kb.ld(xng2[:, 1, :], xn_sp[2 * g + 1])
        kb.wait("tensor", "LD")
        for c in range(2):
            for ob in range(2):
                b = 2 * c + ob
                kb.use_banks([b])
                for ec in range(16):
                    n_mm = kb.mm(BANK(b), tg2[:, ec, c * 128:(c + 1) * 128],
                                 wo_v[:, ec, ob * 512:(ob + 1) * 512],
                                 start=(ec == 0), stop=(ec == 15), sig="MM" if ec == 15 else None)
                kb.wait("vector", "MM", n_mm)
                sl = slice(ob * 512, (ob + 1) * 512)
                kb.v_tt(y1g[:, c, sl], BANK(b), xg2[:, c, sl], OP.add)
                kb.v_tt(y1g[:, c, sl], y1g[:, c, sl], xng2[:, c, sl], OP.add)
                n_v = kb.v_tt(y1g[:, c, sl], y1g[:, c, sl], borep[:, sl], OP.add, sig="V")
                kb.evac_done([b], "V", n_v)
        kb.wait("gpsimd", "V")
        kb.st(y1_sp[2 * g], y1g[:, 0, :])
        kb.st(y1_sp[2 * g + 1], y1g[:, 1, :])

    kb.barrier()
    quantize_weight(w2t_d.rearrange("(k p) dd -> k p dd", p=128), 32, D,
                    lambda k: qw2_v[:, k, :], 2)

    kb.barrier()
    kb.ld(b2orep[:], b2o_rep_d[:, :])
    y_r = y_d.rearrange("(t p) d -> t p d", p=128)
    for g in range(NG):
        for c in range(2):
            idx = 2 * g + c
            kb.barrier()
            kb.ld(qx2f[:], qx2_sp[idx])
            kb.ld(y1f[:], y1_sp[idx])
            kb.wait("sync", "LD")
            kb.wait("vector", "LD")
            kb.trdma(qx2T[:], qx2f[:])
            kb.v_ts(svec[:, 5:6], sc2all[:, idx:idx + 1], wsv[:, 2:3], 1.0 / 127.0,
                    OP.mult, OP.mult, sig="V")
            kb.wait("vector", "V")
            kb.wait("tensor", "TT")
            kb.wait("tensor", "V")
            for ob in range(2):
                b = ob
                kb.use_banks([b])
                for hk in range(32):
                    n_mm = kb.mm(BANK(b), qx2T[:, hk, :],
                                 qw2_v[:, hk, ob * 512:(ob + 1) * 512],
                                 start=(hk == 0), stop=(hk == 31), sig="MM" if hk == 31 else None)
                kb.wait("vector", "MM", n_mm)
                sl = slice(ob * 512, (ob + 1) * 512)
                kb.v_stt(youtf[:, sl], BANK(b), svec[:, 5:6], b2orep[:, sl],
                         OP.mult, OP.add)
                n_v = kb.v_tt(youtf[:, sl], youtf[:, sl], y1f[:, sl], OP.add, sig="V")
                kb.evac_done([b], "V", n_v)
            kb.wait("gpsimd", "V")
            kb.st(y_r[idx], youtf[:])

    fin = kb.sems["ST"]
    n_fin = kb.cnt["ST"]
    kb.ops["gpsimd"].append(lambda e, s=fin, n=n_fin: e.wait_ge(s, n))

    with nc.Block() as block:
        @block.sync
        def _(sync):
            for fn in kb.ops["sync"]:
                fn(sync)

        @block.tensor
        def _(tensor):
            for fn in kb.ops["tensor"]:
                fn(tensor)

        @block.vector
        def _(vector):
            for fn in kb.ops["vector"]:
                fn(vector)

        @block.scalar
        def _(scalar):
            for fn in kb.ops["scalar"]:
                fn(scalar)

        @block.gpsimd
        def _(gpsimd):
            for fn in kb.ops["gpsimd"]:
                fn(gpsimd)

    return nc


def _rel_bias_T(rel_emb):
    NUM_BUCKETS, MAX_DIST = 32, 128
    pos = np.arange(GS)
    n = -(pos[None, :] - pos[:, None])
    nb = NUM_BUCKETS // 2
    ret = (n < 0).astype(np.int32) * nb
    n = np.abs(n)
    max_exact = nb // 2
    is_small = n < max_exact
    val_large = max_exact + (
        np.log(np.maximum(n, 1).astype(np.float32) / max_exact)
        / math.log(MAX_DIST / max_exact) * (nb - max_exact)
    ).astype(np.int32)
    val_large = np.minimum(val_large, nb - 1)
    bucket = ret + np.where(is_small, n, val_large)
    bias = np.asarray(rel_emb)[bucket, 0]
    return np.ascontiguousarray(bias.T.astype(np.float32))


def _rep(v, n=128):
    v = np.asarray(v, np.float32)
    return np.ascontiguousarray(np.broadcast_to(v, (n, v.shape[0])))


_NC_CACHE = {}
_LAST_RES = {}


def _get_nc():
    if "nc" not in _NC_CACHE:
        _NC_CACHE["nc"] = build_program()
    return _NC_CACHE["nc"]


def kernel(L0_attn_x, L0_attn_norm_g, L0_attn_norm_b, L0_attn_W_h, L0_attn_b_h,
           L0_attn_W_qk, L0_attn_b_qk, L0_attn_qk_gamma, L0_attn_qk_beta,
           L0_attn_rel_emb, L0_attn_W_o, L0_attn_b_o,
           L0_ffn_ln1_g, L0_ffn_ln1_b, L0_ffn_w1, L0_ffn_b1,
           L0_ffn_ln2_g, L0_ffn_ln2_b, L0_ffn_w2, L0_ffn_b2):
    f32 = lambda a: np.ascontiguousarray(np.asarray(a, dtype=np.float32))
    x = f32(L0_attn_x)
    W_h = f32(L0_attn_W_h)
    shared = {
        "wqk": f32(L0_attn_W_qk),
        "whv": np.ascontiguousarray(W_h[:, :E]),
        "whg": np.ascontiguousarray(W_h[:, E:]),
        "wo": f32(L0_attn_W_o),
        "w1t": np.ascontiguousarray(f32(L0_ffn_w1).T),
        "w2t": np.ascontiguousarray(f32(L0_ffn_w2).T),
        "bqk": f32(L0_attn_b_qk).reshape(QKD, 1),
        "gqk": np.ascontiguousarray(f32(L0_attn_qk_gamma).T),
        "betaqk": np.ascontiguousarray(f32(L0_attn_qk_beta).T),
        "bhv_rep": _rep(f32(L0_attn_b_h)[:E]),
        "bhg_cols": np.ascontiguousarray(f32(L0_attn_b_h)[E:].reshape(16, 128).T),
        "biasT": _rel_bias_T(f32(L0_attn_rel_emb)),
        "ident": np.eye(128, dtype=np.float32),
        "ones128": np.ones((128, 128), dtype=np.float32),
        "g2rep": _rep(f32(L0_attn_norm_g)),
        "b2rep": _rep(f32(L0_attn_norm_b)),
        "g1rep": _rep(f32(L0_ffn_ln1_g)),
        "b1r": _rep(f32(L0_ffn_ln1_b)),
        "b1rep": _rep(f32(L0_ffn_b1)),
        "g4rep": _rep(f32(L0_ffn_ln2_g)),
        "b4rep": _rep(f32(L0_ffn_ln2_b)),
        "b2o_rep": _rep(f32(L0_ffn_b2)),
        "bo_rep": _rep(f32(L0_attn_b_o)),
    }
    in_maps = []
    for c in range(8):
        b, h = c // 2, c % 2
        m = dict(shared)
        m["x"] = np.ascontiguousarray(x[b, h * TOK:(h + 1) * TOK, :])
        in_maps.append(m)

    nc = _get_nc()
    res = run_bass_kernel_spmd(nc, in_maps, core_ids=list(range(8)))
    _LAST_RES["res"] = res
    out = np.empty((4, 4096, D), dtype=np.float32)
    for c in range(8):
        b, h = c // 2, c % 2
        out[b, h * TOK:(h + 1) * TOK, :] = res.results[c]["y"]
    return out
